# version 14
# baseline (speedup 1.0000x reference)
import contextlib
from collections import defaultdict

import numpy as np
import concourse.bass as bass
import concourse.mybir as mybir
from concourse.bass_utils import run_bass_kernel_spmd

F32 = mybir.dt.float32
BF16 = mybir.dt.bfloat16
AF = mybir.ActivationFunctionType
ALU = mybir.AluOpType
AX = mybir.AxisListType

T = 1024
C = 2048
CC = 16
H = 16
TC = 8
NWCOL = 9
NWKQ = 4

SEMNAMES = ["ld", "sx", "st", "te", "tkv", "tg", "tq", "tu", "ty", "tf",
            "ve", "se", "cc"]


class Ctx:
    def __init__(self, nc, sems, cur):
        self.nc = nc
        self.sems = sems
        self.cur = cur
        self.cnt = defaultdict(int)

    def emit(self, e, fn, *a, sem=None, inc=1, **k):
        i = getattr(getattr(self.nc, e), fn)(*a, **k) if self.cur == e else None
        if sem is not None:
            self.cnt[sem] += inc
            if i is not None:
                i.then_inc(self.sems[sem], inc)
        return self.cnt[sem] if sem is not None else None

    def wait(self, e, sem, val):
        if val <= 0:
            return
        if self.cur == e:
            getattr(self.nc, e).wait_ge(self.sems[sem], val)

    def gdma(self, out, in_):
        return self.emit("gpsimd", "dma_start", sem="ld", inc=16, out=out, in_=in_)

    def gdummy(self):
        return self.emit("gpsimd", "dma_start", sem="ld", inc=16,
                         out=self.dumap, in_=self.dumsrc)

    def sdma(self, out, in_, sem):
        return self.emit("sync", "dma_start", sem=sem, inc=16, out=out, in_=in_)

    def mm(self, out, lhsT, rhs, start, stop, sem=None):
        return self.emit("tensor", "matmul", out, lhsT, rhs,
                         sem=sem, start=start, stop=stop)

    def vspace(self, n=1):
        for _ in range(n):
            self.emit("vector", "tensor_copy", self.spcv, self.spcsrc)

    def sspace(self, n=1):
        for _ in range(n):
            self.emit("scalar", "copy", self.spcs, self.spcsrc)

    def vmark(self, sem="ve", n=2):
        for _ in range(n - 1):
            self.emit("vector", "tensor_copy", self.spcv, self.spcsrc)
        return self.emit("vector", "tensor_copy", self.spcv, self.spcsrc,
                         sem=sem)

    def smark(self, sem="se", n=2):
        for _ in range(n - 1):
            self.emit("scalar", "copy", self.spcs, self.spcsrc)
        return self.emit("scalar", "copy", self.spcs, self.spcsrc, sem=sem)


def build_program(taps=()):
    nc = bass.Bass("TRN2", target_bir_lowering=False)

    dr = {}
    dr["xT"] = nc.dram_tensor("xT", [C, T], F32, kind="ExternalInput").ap()
    for n in ("wqT", "wkT", "wvT", "woT", "wupT", "wdT", "we"):
        dr[n] = nc.dram_tensor(n, [C, C], F32, kind="ExternalInput").ap()
    dr["wgT"] = nc.dram_tensor("wgT", [C, 8], F32, kind="ExternalInput").ap()
    dr["identf"] = nc.dram_tensor("identf", [128, 128], F32, kind="ExternalInput").ap()
    dr["bsel"] = nc.dram_tensor("bsel", [128, 2], F32, kind="ExternalInput").ap()
    dr["outT"] = nc.dram_tensor("outT", [C, T], F32, kind="ExternalOutput").ap()
    cc_in = nc.dram_tensor("cc_in", [128, 2 * 2064], F32).ap()
    cc_out = nc.dram_tensor("cc_out", [128, 2 * 2064], F32).ap()

    tshapes = {"xn": ([128, CC * T], BF16), "k": ([128, TC * C], BF16),
               "v": ([128, TC * C], BF16), "kvev": ([128, 2064], F32),
               "kvsel": ([128, 2064], F32), "q": ([128, CC * T], BF16),
               "h1": ([128, CC * T], BF16), "G": ([128, T], F32),
               "y": ([128, CC * T], BF16), "ltok": ([128, 64], F32),
               "gtok": ([128, 8], F32), "krep": ([128, H * 128], BF16),
               "ccin": ([128, 2 * 2064], F32), "ccout": ([128, 2 * 2064], F32),
               "stg": ([128, 2064], F32)}
    tapdr = {tp: nc.dram_tensor("dbg_" + tp, tshapes[tp][0], tshapes[tp][1],
                                kind="ExternalOutput").ap() for tp in taps}

    stack = contextlib.ExitStack()
    with stack:
        def sb(name, shape, dt, st=stack):
            return st.enter_context(nc.sbuf_tensor(name, shape, dt))

        regA0 = contextlib.ExitStack()
        xstg = sb("xstg", [128, 8 * T], F32, regA0)
        regA0.close()
        xn = sb("xn", [128, CC * T], BF16)
        kvbf = sb("kvbf", [128, H * 128], BF16)
        krep = sb("krep", [128, H * 128], BF16)
        kvsel = sb("kvsel", [128, 2064], F32)
        Gt = sb("Gt", [128, T], F32)
        wg = sb("wg", [128, CC * 8], BF16)
        onesb = sb("onesb", [128, 128], BF16)
        identf = sb("identf_sb", [128, 128], F32)
        onesf = sb("onesf", [1, 128], F32)
        bselT = sb("bselT", [128, 2], F32)
        gate = sb("gate", [128, 128], F32)
        fms = sb("fms", [128, 4 * 512], F32)
        grow = Gt[0:1, 0:T]
        epsb = sb("epsb", [128, 1], F32)
        spc = sb("spc", [128, 16], F32)
        dum = sb("dum", [128, 32], F32)
        psb = [stack.enter_context(nc.psum_tensor(f"ps{i}", [128, 512], F32))
               for i in range(8)]
        sems = {n: stack.enter_context(nc.semaphore(n)) for n in SEMNAMES}

        regB = contextlib.ExitStack()
        wkq = sb("wkq", [128, NWKQ * CC * 512], BF16, regB)
        kvev = sb("kvev", [128, 2064], F32, regB)
        stg = sb("stg", [128, 2064], F32, regB)
        regC1 = contextlib.ExitStack()
        xbf = sb("xbf", [128, CC * T], BF16, regC1)
        xsq = sb("xsq", [128, 2 * T], BF16, regC1)
        Rb = sb("Rb", [128, T], F32, regC1)
        MRb = sb("MRb", [128, T], F32, regC1)
        tmpr = sb("tmpr", [128, 2 * T], F32, regC1)
        regC1.close()
        regC2 = contextlib.ExitStack()
        ksb = sb("ksb", [128, TC * C], BF16, regC2)
        vsb = sb("vsb", [128, TC * 2064], BF16, regC2)
        regC2.close()
        regB.close()
        b_orig = NWKQ * CC * 512 * 2 + 2064 * 4 + 2064 * 4
        b_new = NWCOL * C * 2 + 2064 * 4 + CC * T * 2
        padb = b_orig - b_new
        assert padb >= 0 and padb % 32 == 0, padb
        wcol = sb("wcol", [128, NWCOL * C], BF16)
        scr9 = sb("scr9", [128, 2064], F32)
        ysb = sb("ysb", [128, CC * T], BF16)
        if padb:
            sb("padB", [128, padb // 4], F32)
        qsb = sb("qsb", [128, CC * T], BF16)
        h1sb = sb("h1sb", [128, CC * T], BF16)
        addr = lambda h: nc.lookup_mloc(h).addr
        assert addr(xstg) == addr(xn), (addr(xstg), addr(xn))
        assert addr(wcol) == addr(wkq), (addr(wcol), addr(wkq))
        assert addr(qsb) == addr(ksb), (addr(qsb), addr(ksb))
        assert addr(h1sb) == addr(vsb), (addr(h1sb), addr(vsb))

        for n in SEMNAMES:
            nc.sync.sem_clear(sems[n])

        def emit(ctx: Ctx):
            ctx.emit("vector", "memset", onesb[:], 1.0)
            ctx.emit("vector", "memset", epsb[:], 1e-6)
            consts_ve = ctx.emit("vector", "memset", onesf[:], 1.0, sem="ve")
            ctx.sdma(identf[:], dr["identf"], "sx")
            ctx.sdma(bselT[:], dr["bsel"], "sx")
            ctx.sdma(identf[:], dr["identf"], "sx")
            bsel_sx = ctx.sdma(bselT[:], dr["bsel"], "sx")

            def wg_load():
                return ctx.gdma(wg[:].rearrange("p (cc e) -> p cc e", cc=CC),
                                dr["wgT"].rearrange("(cc p) e -> p cc e", p=128))

            wg_load()
            order = list(range(7, CC)) + list(range(7))
            gp_chunks = [cc for i, cc in enumerate(order) if i % 2 == 0]
            sy_chunks = [cc for i, cc in enumerate(order) if i % 2 == 1]
            sslot = {cc: j for j, cc in enumerate(sy_chunks)}
            xv = {}
            for cc in gp_chunks:
                ctx.gdma(xbf[:, cc * T:(cc + 1) * T],
                         dr["xT"][cc * 128:(cc + 1) * 128, :])
            for cc in sy_chunks:
                ctx.sdma(xstg[:, sslot[cc] * T:(sslot[cc] + 1) * T],
                         dr["xT"][cc * 128:(cc + 1) * 128, :], "sx")
            for cc in gp_chunks:
                xv[cc] = ctx.gdma(xbf[:, cc * T:(cc + 1) * T],
                                  dr["xT"][cc * 128:(cc + 1) * 128, :])
            for cc in sy_chunks:
                xv[cc] = ctx.sdma(xstg[:, sslot[cc] * T:(sslot[cc] + 1) * T],
                                  dr["xT"][cc * 128:(cc + 1) * 128, :], "sx")
            wg_ld = wg_load()
            sx_tail = ctx.sdma(dum[:], dr["identf"][:, 0:32], "sx")
            xwait = {}
            for i, cc in enumerate(gp_chunks):
                nxt = xv[gp_chunks[i + 1]] if i + 1 < len(gp_chunks) else wg_ld
                xwait[cc] = ("ld", nxt)
            for i, cc in enumerate(sy_chunks):
                nxt = xv[sy_chunks[i + 1]] if i + 1 < len(sy_chunks) else sx_tail
                xwait[cc] = ("sx", nxt)

            sq_se = {}
            for i, cc in enumerate(order):
                ctx.wait("scalar", *xwait[cc])
                if i >= 2:
                    ctx.wait("scalar", "te", i - 1)
                if cc in sslot:
                    j = sslot[cc]
                    ctx.emit("scalar", "copy", xbf[:, cc * T:(cc + 1) * T],
                             xstg[:, j * T:(j + 1) * T])
                    ctx.emit("scalar", "activation",
                             xsq[:, (i % 2) * T:(i % 2 + 1) * T],
                             xstg[:, j * T:(j + 1) * T], AF.Square)
                else:
                    ctx.emit("scalar", "activation",
                             xsq[:, (i % 2) * T:(i % 2 + 1) * T],
                             xbf[:, cc * T:(cc + 1) * T], AF.Square)
                sq_se[i] = ctx.smark(n=4)

            ctx.wait("tensor", "ve", consts_ve)
            for i, cc in enumerate(order):
                ctx.wait("tensor", "se", sq_se[i])
                ctx.wait("tensor", *xwait[cc])
                for tt in range(2):
                    ctx.mm(psb[tt][0:1, :], onesb[:, 0:1],
                           xbf[:, cc * T + tt * 512: cc * T + (tt + 1) * 512],
                           start=(i == 0), stop=(i == CC - 1))
                    ctx.mm(psb[2 + tt][0:1, :], onesb[:, 0:1],
                           xsq[:, (i % 2) * T + tt * 512:(i % 2) * T + (tt + 1) * 512],
                           start=(i == 0), stop=(i == CC - 1),
                           sem="te" if tt == 1 else None)
            stats_te = ctx.cnt["te"]
            assert stats_te == CC

            A = fms[0:1, 0:T]
            B = fms[0:1, T:2 * T]
            TMP = MRb[0:1, 0:T]
            ctx.wait("vector", "te", stats_te)
            for tt in range(2):
                ctx.emit("vector", "tensor_scalar_mul",
                         A[0:1, tt * 512:(tt + 1) * 512], psb[tt][0:1, :], 1.0 / C)
            for tt in range(2):
                ctx.emit("vector", "tensor_scalar_mul",
                         B[0:1, tt * 512:(tt + 1) * 512],
                         psb[2 + tt][0:1, :], 1.0 / C)
            ctx.emit("vector", "tensor_mul", TMP, A, A)
            ctx.vspace(2)
            ctx.emit("vector", "tensor_sub", B, B, TMP)
            ctx.vspace(2)
            ctx.emit("vector", "tensor_scalar_add", B, B, 1e-5)
            var_ve = ctx.vmark(n=4)
            ctx.wait("scalar", "ve", var_ve)
            ctx.emit("scalar", "activation", B, B, AF.Sqrt)
            sd_se = ctx.smark(n=4)
            ctx.wait("vector", "se", sd_se)
            ctx.emit("vector", "reciprocal", B, B)
            ctx.vspace(2)
            ctx.emit("vector", "tensor_mul", TMP, A, B)
            rows_ve = ctx.vmark(n=4)

            ctx.wait("tensor", "ve", rows_ve)
            for tt in range(2):
                ctx.mm(psb[4 + tt][:, :], onesf[:],
                       B[0:1, tt * 512:(tt + 1) * 512], True, True)
                ctx.mm(psb[6 + tt][:, :], onesf[:],
                       TMP[0:1, tt * 512:(tt + 1) * 512], True, True,
                       sem="te" if tt == 1 else None)
            bc_te = ctx.cnt["te"]
            assert bc_te == CC + 1
            ctx.wait("scalar", "te", bc_te)
            for tt in range(2):
                ctx.emit("scalar", "copy", Rb[:, tt * 512:(tt + 1) * 512],
                         psb[4 + tt][:, :])
                ctx.emit("scalar", "copy", MRb[:, tt * 512:(tt + 1) * 512],
                         psb[6 + tt][:, :])
            rbmrb_se = ctx.smark(n=4)

            ctx.wait("vector", "se", rbmrb_se)
            xn_tc = {}
            for tc in range(TC):
                rbs = Rb[:, tc * 128:(tc + 1) * 128]
                mrs = MRb[:, tc * 128:(tc + 1) * 128]
                for cp in range(CC // 2):
                    c0, c1 = 2 * cp, 2 * cp + 1
                    o0 = c0 * T + tc * 128
                    o1 = c1 * T + tc * 128
                    ctx.emit("vector", "tensor_mul", tmpr[:, 0:128],
                             xbf[:, o0:o0 + 128], rbs)
                    ctx.emit("vector", "tensor_mul", tmpr[:, 128:256],
                             xbf[:, o1:o1 + 128], rbs)
                    ctx.emit("vector", "tensor_sub", xn[:, o0:o0 + 128],
                             tmpr[:, 0:128], mrs)
                    ctx.emit("vector", "tensor_sub", xn[:, o1:o1 + 128],
                             tmpr[:, 128:256], mrs)
                xn_tc[tc] = ctx.vmark()
            xn_ve = xn_tc[TC - 1]
            for tc in range(TC):
                ctx.emit("vector", "memset",
                         vsb[:, tc * 2064:(tc + 1) * 2064].rearrange(
                             "p (h e) -> p h e", h=H)[:, :, 128:129], 1.0)
            vones_ve = ctx.vmark()
            if "xn" in tapdr:
                ctx.wait("sync", "ve", xn_ve)
                ctx.sdma(tapdr["xn"], xn[:], "st")

            P2TE0 = CC + 1
            shld = {}

            def p2_sub_done(w, sh):
                return P2TE0 + (w * 4 + sh + 1) * TC

            shcush = {}
            for w, wname in enumerate(("wkT", "wvT")):
                for sh in range(4):
                    bi = w * 4 + sh
                    if bi >= NWKQ:
                        pw, psh = divmod(bi - NWKQ, 4)
                        ctx.wait("gpsimd", "te", p2_sub_done(pw, psh))
                        ctx.gdummy()
                    slot = bi % NWKQ
                    src = dr[wname][:, sh * 512:(sh + 1) * 512].rearrange(
                        "(cc p) o -> p cc o", p=128)
                    dst = wkq[:, slot * CC * 512:(slot + 1) * CC * 512].rearrange(
                        "p (cc o) -> p cc o", cc=CC)
                    shld[(w, sh)] = ctx.gdma(dst, src)
                    shcush[(w, sh)] = ctx.gdma(dum[:], dr["identf"][:, 0:32])
            tail_kv_ld = ctx.cnt["ld"]

            bankfree = {}
            kadd_ve = {}
            NROT = 6
            for G in range(64):
                w, sh, tc = G // 32, (G // 8) % 4, G % 8
                ctx.wait("tensor", "ld", shcush[(w, sh)])
                if G == 0:
                    ctx.wait("tensor", "ld", wg_ld)
                if w == 0 and sh < 2:
                    ctx.wait("tensor", "ve", xn_tc[tc])
                if (w, sh, tc) == (0, 2, 0):
                    ctx.wait("tensor", "ve", xn_ve)
                if G >= NROT:
                    sname, v = bankfree[G - NROT]
                    ctx.wait("tensor", sname, v)
                bank = G % NROT
                slot = (w * 4 + sh) % NWKQ
                for cc in range(CC):
                    lhsT = xn[:, cc * T + tc * 128: cc * T + tc * 128 + 128]
                    s0 = (slot * CC + cc) * 512
                    last = (cc == CC - 1)
                    if w == 0 and sh == 0:
                        ctx.mm(psb[7][:, tc * 8:(tc + 1) * 8], lhsT,
                               wg[:, cc * 8:(cc + 1) * 8],
                               start=(cc == 0), stop=last)
                    ctx.mm(psb[bank][:, :], lhsT, wkq[:, s0:s0 + 512],
                           start=(cc == 0), stop=last,
                           sem="te" if last else None)
                gte = ctx.cnt["te"]
                assert gte == P2TE0 + G + 1

                dst_off = tc * C + sh * 512
                if w == 0:
                    ctx.wait("scalar", "te", gte)
                    ctx.wait("scalar", "ve", xn_ve)
                    if G >= 2 and (G - 2) in kadd_ve:
                        ctx.wait("scalar", "ve", kadd_ve[G - 2])
                    base = (G % 2) * 2 * 512
                    rS = fms[:, base:base + 512]
                    pS = fms[:, base + 512:base + 1024]
                    ctx.emit("scalar", "activation", rS, psb[bank][:, :],
                             AF.Relu, scale=-1.0)
                    ctx.emit("scalar", "activation", pS, psb[bank][:, :],
                             AF.Relu)
                    ctx.emit("scalar", "activation", rS, rS, AF.Exp,
                             scale=-1.0)
                    kse = ctx.smark()
                    ctx.wait("vector", "se", kse)
                    ctx.emit("vector", "tensor_add",
                             ksb[:, dst_off:dst_off + 512], rS, pS)
                    kadd_ve[G] = ctx.vmark()
                    bankfree[G] = ("se", kse)
                else:
                    ctx.wait("vector", "te", gte)
                    vdst = vsb[:, tc * 2064 + sh * 4 * 129:
                               tc * 2064 + (sh * 4 + 4) * 129].rearrange(
                        "p (h e) -> p h e", h=4)[:, :, 0:128]
                    vsrc = psb[bank][:, :].rearrange("p (h e) -> p h e", h=4)
                    ctx.emit("vector", "tensor_copy", vdst, vsrc, sem="ve")
                    bankfree[G] = ("ve", ctx.cnt["ve"])
            assert ctx.cnt["te"] == P2TE0 + 64
            kv_inputs_ve = ctx.vmark()

            if "k" in tapdr:
                ctx.wait("sync", "ve", kv_inputs_ve)
                ctx.sdma(tapdr["k"], ksb[:], "st")
            if "v" in tapdr:
                ctx.wait("sync", "ve", kv_inputs_ve)
                ctx.sdma(tapdr["v"], vsb[:], "st")

            LT = gate[:, 0:64]
            M1, M2 = gate[:, 64:72], gate[:, 72:80]
            DF, E2 = gate[:, 80:88], gate[:, 88:96]
            U, GK, MS = gate[:, 96:104], gate[:, 104:112], gate[:, 112:120]
            gate_te = P2TE0 + 8
            ctx.wait("vector", "te", gate_te)
            ltok_ve = ctx.emit("vector", "tensor_copy", LT, psb[7][:, 0:64],
                               sem="ve")
            if "ltok" in tapdr:
                ctx.wait("sync", "ve", ltok_ve)
                ctx.sdma(tapdr["ltok"], gate[:, 0:64], "st")
            MSt = [fms[:, 8 * tc: 8 * (tc + 1)] for tc in range(TC)]
            for tc in range(TC):
                ctx.emit("vector", "reduce_max", M1[:, tc:tc + 1],
                         gate[:, tc * 8:(tc + 1) * 8], AX.X)
            for tc in range(TC):
                ctx.emit("vector", "tensor_scalar", MSt[tc],
                         gate[:, tc * 8:(tc + 1) * 8], M1[:, tc:tc + 1],
                         None, ALU.is_ge)
            for tc in range(TC):
                ctx.emit("vector", "scalar_tensor_tensor", MSt[tc], MSt[tc],
                         -1e30, gate[:, tc * 8:(tc + 1) * 8],
                         ALU.mult, ALU.add)
            for tc in range(TC):
                ctx.emit("vector", "reduce_max", M2[:, tc:tc + 1], MSt[tc], AX.X)
            ctx.vspace(2)
            ctx.emit("vector", "tensor_sub", DF, M2, M1)
            df_ve = ctx.vmark(n=4)
            ctx.wait("scalar", "ve", df_ve)
            ctx.emit("scalar", "activation", E2, DF, AF.Exp)
            e2_se = ctx.smark(n=4)
            ctx.wait("vector", "se", e2_se)
            ctx.emit("vector", "tensor_scalar_add", U, E2, 1.0)
            ctx.vspace(2)
            ctx.emit("vector", "reciprocal", U, U)
            ctx.vspace(2)
            ctx.emit("vector", "scalar_tensor_tensor", GK, E2, 1.0, U,
                     ALU.add, ALU.mult)
            gtok_ve = ctx.vmark(n=4)
            if "gtok" in tapdr:
                ctx.wait("sync", "ve", gtok_ve)
                ctx.sdma(tapdr["gtok"], gate[:, 104:112], "st")
            ctx.wait("tensor", "ve", kv_inputs_ve)
            ctx.wait("tensor", "ve", vones_ve)
            kvev_ve = {}
            for h in range(H):
                bank = h % 6
                if h >= 6:
                    ctx.wait("tensor", "ve", kvev_ve[h - 6])
                for tc in range(TC):
                    lhsT = ksb[:, tc * C + h * 128: tc * C + (h + 1) * 128]
                    ctx.mm(psb[bank][:, 0:129], lhsT,
                           vsb[:, tc * 2064 + h * 129: tc * 2064 + (h + 1) * 129],
                           start=(tc == 0), stop=(tc == TC - 1),
                           sem="tkv" if tc == TC - 1 else None)
                gkv = ctx.cnt["tkv"]
                ctx.wait("vector", "tkv", gkv)
                ctx.emit("vector", "tensor_copy",
                         kvev[:, h * 129:(h + 1) * 129], psb[bank][:, 0:129],
                         sem="ve")
                kvev_ve[h] = ctx.cnt["ve"]
                if h in (5, 11):
                    hf = 0 if h == 5 else 1
                    if hf == 0:
                        ctx.wait("tensor", "ve", gtok_ve)
                        ctx.wait("tensor", "sx", bsel_sx)
                        ctx.wait("tensor", "ve", ltok_ve)
                    else:
                        ctx.wait("tensor", "se", grow_se)
                    for tci in range(4):
                        tc = hf * 4 + tci
                        ctx.mm(psb[7][0:1, tci * 128:(tci + 1) * 128],
                               gate[:, 104 + tc:105 + tc], identf[:], True, True,
                               sem="tg" if tci == 3 else None)
                    gid = ctx.cnt["tg"]
                    ctx.wait("scalar", "tg", gid)
                    ctx.emit("scalar", "copy",
                             grow[:, hf * 512:(hf + 1) * 512], psb[7][0:1, :])
                    grow_se = ctx.smark(n=4)
            assert ctx.cnt["tkv"] == H
            kvev_all_ve = ctx.vmark()
            if "kvev" in tapdr:
                ctx.wait("sync", "ve", kvev_all_ve)
                ctx.sdma(tapdr["kvev"], kvev[:], "st")


            ctx.wait("tensor", "se", grow_se)
            ctx.wait("tensor", "ve", kvev_all_ve)
            ctx.mm(psb[4][:, :], onesf[:], grow[:, 0:512], True, True)
            ctx.mm(psb[5][:, :], onesf[:], grow[:, 512:1024], True, True,
                   sem="tg")
            gb_te = ctx.cnt["tg"]
            assert gb_te == 3
            ctx.wait("scalar", "tg", gb_te)
            ctx.emit("scalar", "copy", Gt[:, 0:512], psb[4][:, :])
            ctx.emit("scalar", "copy", Gt[:, 512:1024], psb[5][:, :])
            Gt_se = ctx.smark(n=4)
            if "G" in tapdr:
                ctx.wait("sync", "se", Gt_se)
                ctx.sdma(tapdr["G"], Gt[:], "st")

            blocks = ([("wqT", oc) for oc in range(CC)]
                      + [("wupT", oc) for oc in range(CC)]
                      + [(nm, oc) for oc in range(CC)
                         for nm in ("woT", "wdT", "we")])
            nq = nu = CC

            def block_consumed(bi):
                if bi < nq:
                    return ("tq", 2 * (bi + 1))
                if bi < nq + nu:
                    return ("tu", 2 * (bi - nq + 1))
                return ("tf", 2 * ((bi - nq - nu) // 3 + 1))

            blk_ld = {}
            cc_done = kvsel_ld = h0tmp_ld = None
            stg0_ve = stg1_ve = None
            vdone_te = p2_sub_done(1, 3)
            def wcol_load(bi):
                nm, oc = blocks[bi]
                src = dr[nm][:, oc * 128:(oc + 1) * 128].rearrange(
                    "(cc p) o -> p cc o", p=128)
                dst = wcol[:, (bi % NWCOL) * C:(bi % NWCOL + 1) * C].rearrange(
                    "p (cc o) -> p cc o", cc=CC)
                return ctx.gdma(dst, src)

            for bi, (nm, oc) in enumerate(blocks):
                if bi < NWCOL:
                    sh_owner = [0, 0, 0, 0, 1, 1, 1, 1, 2][bi]
                    ctx.wait("gpsimd", "te", p2_sub_done(1, sh_owner))
                    if bi == 0:
                        ctx.gdummy()
                if bi >= NWCOL:
                    sname, v = block_consumed(bi - NWCOL)
                    ctx.wait("gpsimd", sname, v)
                    ctx.gdummy()
                blk_ld[bi] = wcol_load(bi)
                if bi == 6:
                    ctx.wait("vector", "sx", bsel_sx)
                    ctx.emit("vector", "tensor_scalar_mul", stg[:],
                             kvev[:], bselT[:, 0:1], sem=None)
                    stg0_ve = ctx.vmark()
                    ctx.wait("gpsimd", "ve", stg0_ve)
                    ctx.gdummy()
                    ctx.gdma(cc_in[:, 0:2064], stg[:])
                    cid0 = ctx.gdma(cc_in[:, 0:2064], stg[:])
                    ctx.wait("vector", "ld", cid0)
                    ctx.emit("vector", "tensor_scalar_mul", stg[:],
                             kvev[:], bselT[:, 1:2], sem=None)
                    stg1_ve = ctx.vmark()
                    if "stg" in tapdr:
                        ctx.wait("sync", "ve", stg1_ve)
                        ctx.sdma(tapdr["stg"], stg[:], "st")
                    ctx.wait("gpsimd", "ve", stg1_ve)
                    ctx.gdma(cc_in[:, 2064:4128], stg[:])
                    cid1 = ctx.gdma(cc_in[:, 2064:4128], stg[:])
                    ctx.wait("gpsimd", "ld", cid1)
                    cc_done = ctx.emit("gpsimd", "collective_compute",
                                       "AllReduce", ALU.add,
                                       replica_groups=[list(range(8))],
                                       ins=[cc_in], outs=[cc_out],
                                       sem="cc", inc=1)
                if bi == nq + 1:
                    ctx.wait("gpsimd", "cc", cc_done)
                    ctx.gdummy()
                    ctx.gdma(kvsel[:], cc_out[:, 0:2064])
                    ctx.gdma(scr9[:], cc_out[:, 2064:4128])
                    kvsel_ld = ctx.gdma(kvsel[:], cc_out[:, 0:2064])
                    h0tmp_ld = ctx.gdma(scr9[:], cc_out[:, 2064:4128])
                    if "ccin" in tapdr:
                        ctx.gdma(tapdr["ccin"], cc_in)
                    if "ccout" in tapdr:
                        ctx.gdma(tapdr["ccout"], cc_out)
            tail_ld = wg_load()

            qadd_ve = {}
            qp3_se = {}
            for j in range(32):
                oc, tt = j // 2, j % 2
                bi = oc
                cush = blk_ld[bi + 1] if bi + 1 < len(blocks) else tail_ld
                ctx.wait("tensor", "ld", cush)
                if j >= 4:
                    ctx.wait("tensor", "se", qp3_se[j - 4])
                else:
                    sname, v = bankfree[60 + j]
                    ctx.wait("tensor", sname, v)
                bank = j % 4
                for cc in range(CC):
                    lhsT = wcol[:, (bi % NWCOL) * C + cc * 128:
                                (bi % NWCOL) * C + (cc + 1) * 128]
                    ctx.mm(psb[bank][:, :], lhsT,
                           xn[:, cc * T + tt * 512: cc * T + (tt + 1) * 512],
                           start=(cc == 0), stop=(cc == CC - 1),
                           sem="tq" if cc == CC - 1 else None)
                jte = ctx.cnt["tq"]
                ctx.wait("scalar", "tq", jte)
                if j >= 2:
                    ctx.wait("scalar", "ve", qadd_ve[j - 2])
                base = (j % 2) * 2 * 512
                rS = fms[:, base:base + 512]
                pS = fms[:, base + 512:base + 1024]
                ctx.emit("scalar", "activation", rS, psb[bank][:, :],
                         AF.Relu, scale=-1.0)
                ctx.emit("scalar", "activation", pS, psb[bank][:, :], AF.Relu)
                ctx.emit("scalar", "activation", rS, rS, AF.Exp, scale=-1.0)
                qp3_se[j] = ctx.smark()
                ctx.wait("vector", "se", qp3_se[j])
                qadd_ve[j] = ctx.emit(
                    "vector", "tensor_add",
                    qsb[:, oc * T + tt * 512: oc * T + (tt + 1) * 512],
                    rS, pS, sem="ve")
                if j == 27:
                    ctx.wait("vector", "ld", h0tmp_ld)
                    ctx.emit("vector", "tensor_scalar_mul", kvsel[:], kvsel[:],
                             bselT[:, 0:1])
                    ctx.vspace(2)
                    sel_ve = ctx.emit("vector", "scalar_tensor_tensor", kvsel[:],
                                      scr9[:], bselT[:, 1:2], kvsel[:],
                                      ALU.mult, ALU.add, sem="ve")
                    ctx.vspace(2)
                    ctx.emit("vector", "tensor_copy",
                             kvbf[:].rearrange("p (h e) -> p h e", h=H),
                             kvsel[:].rearrange("p (h e) -> p h e", h=H)[:, :, 0:128])
                    for h in range(H):
                        ctx.emit("vector", "tensor_scalar_mul",
                                 krep[:, h * 128:(h + 1) * 128], onesb[:],
                                 kvsel[:, h * 129 + 128: h * 129 + 129])
                    kv_ready_ve = ctx.vmark(n=4)
                    if "kvsel" in tapdr:
                        ctx.wait("sync", "ve", kv_ready_ve)
                        ctx.sdma(tapdr["kvsel"], kvsel[:], "st")
                    if "krep" in tapdr:
                        ctx.wait("sync", "ve", kv_ready_ve)
                        ctx.sdma(tapdr["krep"], krep[:], "st")


            assert ctx.cnt["tq"] == 32
            q_ve = ctx.cnt["ve"]
            if "q" in tapdr:
                ctx.wait("sync", "ve", q_ve)
                ctx.sdma(tapdr["q"], qsb[:], "st")

            h1_se = {}
            yev_ve = {}
            for j in range(32):
                oc, tt = j // 2, j % 2
                bi = nq + oc
                cush = blk_ld[bi + 1] if bi + 1 < len(blocks) else tail_ld
                ctx.wait("tensor", "ld", cush)
                if j >= 4:
                    ctx.wait("tensor", "se", h1_se[j - 4])
                else:
                    ctx.wait("tensor", "se", qp3_se[28 + j])
                bank = j % 4
                for cc in range(CC):
                    lhsT = wcol[:, (bi % NWCOL) * C + cc * 128:
                                (bi % NWCOL) * C + (cc + 1) * 128]
                    ctx.mm(psb[bank][:, :], lhsT,
                           xn[:, cc * T + tt * 512: cc * T + (tt + 1) * 512],
                           start=(cc == 0), stop=(cc == CC - 1),
                           sem="tu" if cc == CC - 1 else None)
                jte = ctx.cnt["tu"]
                ctx.wait("scalar", "tu", jte)
                h1_se[j] = ctx.emit(
                    "scalar", "activation",
                    h1sb[:, oc * T + tt * 512: oc * T + (tt + 1) * 512],
                    psb[bank][:, :], AF.Relu, sem="se")

                h, ptt = j // 2, j % 2
                p = j % 2
                if j == 0:
                    ctx.wait("tensor", "ve", kv_ready_ve)
                    ctx.wait("tensor", "se", Gt_se)
                if j >= 2:
                    ctx.wait("tensor", "ve", yev_ve[j - 2])
                qs = qsb[:, h * T + ptt * 512: h * T + (ptt + 1) * 512]
                ctx.mm(psb[4 + 2 * p][:, :], krep[:, h * 128:(h + 1) * 128],
                       qs, True, True)
                ite = ctx.mm(psb[5 + 2 * p][:, :],
                             kvbf[:, h * 128:(h + 1) * 128], qs, True, True,
                             sem="ty")
                ctx.wait("scalar", "ty", ite)
                zb = scr9[:, p * 512:(p + 1) * 512]
                ctx.emit("scalar", "add", zb, psb[4 + 2 * p][:, :], epsb[:])
                zse = ctx.smark()
                ctx.wait("vector", "se", zse)
                ctx.emit("vector", "reciprocal", zb, zb)
                ctx.vspace(1)
                yev_ve[j] = ctx.emit(
                    "vector", "tensor_mul",
                    ysb[:, h * T + ptt * 512: h * T + (ptt + 1) * 512],
                    psb[5 + 2 * p][:, :], zb, sem="ve")
            assert ctx.cnt["tu"] == 32
            assert ctx.cnt["ty"] == 32
            h1_all_se = ctx.smark()
            y_ve = ctx.vmark()
            if "h1" in tapdr:
                ctx.wait("sync", "se", h1_all_se)
                ctx.sdma(tapdr["h1"], h1sb[:], "st")
            if "y" in tapdr:
                ctx.wait("sync", "ve", y_ve)
                ctx.sdma(tapdr["y"], ysb[:], "st")

            ctx.wait("tensor", "ve", y_ve)
            ctx.wait("tensor", "se", h1_all_se)
            xres_sx = {}
            oev_ve = {}
            ost_st = {}
            for j in range(32):
                oc, tt = j // 2, j % 2
                if j == 0:
                    ctx.wait("sync", "ve", sel_ve)
                    ctx.sdma(dum[:], dr["identf"][:, 0:32], "sx")
                    for rep in range(2):
                        for j2 in range(2):
                            o2, t2 = j2 // 2, j2 % 2
                            xres_sx[j2] = ctx.sdma(
                                scr9[:, 1024 + j2 * 512:1024 + (j2 + 1) * 512],
                                dr["xT"][o2 * 128:(o2 + 1) * 128,
                                         t2 * 512:(t2 + 1) * 512], "sx")
                bq, bd = 2 * (j % 2), 2 * (j % 2) + 1
                if j >= 2:
                    ctx.wait("tensor", "ve", oev_ve[j - 2])
                bio = nq + nu + 3 * oc
                biu, bie = bio + 1, bio + 2
                cush = blk_ld[bie + 1] if bie + 1 < len(blocks) else tail_ld
                ctx.wait("tensor", "ld", cush)
                for cc in range(CC):
                    lhsT = wcol[:, (bio % NWCOL) * C + cc * 128:
                                (bio % NWCOL) * C + (cc + 1) * 128]
                    ctx.mm(psb[bq][:, :], lhsT,
                           ysb[:, cc * T + tt * 512: cc * T + (tt + 1) * 512],
                           start=(cc == 0), stop=False)
                for cc in range(CC):
                    lhsT = wcol[:, (biu % NWCOL) * C + cc * 128:
                                (biu % NWCOL) * C + (cc + 1) * 128]
                    ctx.mm(psb[bq][:, :], lhsT,
                           h1sb[:, cc * T + tt * 512: cc * T + (tt + 1) * 512],
                           start=False, stop=(cc == CC - 1))
                for cc in range(CC):
                    lhsT = wcol[:, (bie % NWCOL) * C + cc * 128:
                                (bie % NWCOL) * C + (cc + 1) * 128]
                    ctx.mm(psb[bd][:, :], lhsT,
                           xn[:, cc * T + tt * 512: cc * T + (tt + 1) * 512],
                           start=(cc == 0), stop=(cc == CC - 1),
                           sem="tf" if cc == CC - 1 else None)
                jte = ctx.cnt["tf"]

                ctx.wait("vector", "tf", jte)
                ctx.wait("vector", "sx", xres_sx[j])
                if j >= 2:
                    ctx.wait("vector", "st", ost_st[j - 2])
                if j == 0:
                    ctx.wait("vector", "se", Gt_se)
                ot = scr9[:, (j % 2) * 512:(j % 2 + 1) * 512]
                gslice = Gt[:, tt * 512:(tt + 1) * 512]
                ft = fms[:, ((j % 2) * 2) * 512:((j % 2) * 2) * 512 + 512]
                ctx.emit("vector", "tensor_add", ot, psb[bq][:, :],
                         scr9[:, 1024 + (j % 2) * 512:1024 + (j % 2 + 1) * 512])
                ctx.emit("vector", "tensor_mul", ft, psb[bd][:, :], gslice)
                ctx.vspace(1)
                ctx.emit("vector", "tensor_add", ot, ot, ft)
                oev_ve[j] = ctx.vmark()

                ctx.wait("sync", "ve", oev_ve[j])
                ctx.sdma(
                    dr["outT"][oc * 128:(oc + 1) * 128, tt * 512:(tt + 1) * 512],
                    ot, "st")
                ost_st[j] = ctx.sdma(
                    dr["outT"][oc * 128:(oc + 1) * 128, tt * 512:(tt + 1) * 512],
                    ot, "st")
                if j + 2 < 32:
                    o2, t2 = (j + 2) // 2, (j + 2) % 2
                    xres_sx[j + 2] = ctx.sdma(
                        scr9[:, 1024 + (j % 2) * 512:1024 + (j % 2 + 1) * 512],
                        dr["xT"][o2 * 128:(o2 + 1) * 128,
                                 t2 * 512:(t2 + 1) * 512], "sx")
            assert ctx.cnt["tf"] == 32
            ctx.wait("sync", "st", ctx.cnt["st"])

        def mkctx(name):
            c = Ctx(nc, sems, name)
            c.spcv = spc[:, 0:8]
            c.spcs = spc[:, 8:16]
            c.spcsrc = onesb[:, 0:8]
            c.dumap = dum[:]
            c.dumsrc = dr["identf"][:, 0:32]
            return c

        with nc.Block() as block:
            block.tensor(lambda _: emit(mkctx("tensor")))
            block.vector(lambda _: emit(mkctx("vector")))
            block.scalar(lambda _: emit(mkctx("scalar")))
            block.sync(lambda _: emit(mkctx("sync")))
            block.gpsimd(lambda _: emit(mkctx("gpsimd")))

    return nc


_NC_CACHE = {}


def _get_nc(taps=()):
    key = tuple(sorted(taps))
    if key not in _NC_CACHE:
        _NC_CACHE[key] = build_program(taps=key)
    return _NC_CACHE[key]


def make_in_maps(inputs):
    x = np.asarray(inputs["L0_attn_x"], dtype=np.float32)
    WqT = np.ascontiguousarray(np.asarray(inputs["L0_attn_Wq"], dtype=np.float32).T)
    WkT = np.ascontiguousarray(np.asarray(inputs["L0_attn_Wk"], dtype=np.float32).T)
    WvT = np.ascontiguousarray(np.asarray(inputs["L0_attn_Wv"], dtype=np.float32).T)
    WoT = np.ascontiguousarray(np.asarray(inputs["L0_attn_Wo"], dtype=np.float32).T)
    WupT = np.ascontiguousarray(np.asarray(inputs["L0_ffn_W_up"], dtype=np.float32).T)
    WdT = np.ascontiguousarray(np.asarray(inputs["L0_ffn_W_down"], dtype=np.float32).T)
    We = np.ascontiguousarray(np.asarray(inputs["L0_ffn_W_expert"], dtype=np.float32))
    WgT = np.ascontiguousarray(np.asarray(inputs["L0_ffn_W_gate"], dtype=np.float32).T)
    identf = np.eye(128, dtype=np.float32)
    xt = x.reshape(8192, C)
    in_maps = []
    for c in range(8):
        bsel = np.zeros((128, 2), dtype=np.float32)
        bsel[:, c // 4] = 1.0
        in_maps.append({
            "xT": np.ascontiguousarray(xt[c * T:(c + 1) * T].T),
            "wqT": WqT, "wkT": WkT, "wvT": WvT, "woT": WoT,
            "wupT": WupT, "wdT": WdT, "we": We, "wgT": WgT,
            "identf": identf, "bsel": bsel,
        })
    return in_maps


def kernel(**inputs):
    nc = _get_nc()
    in_maps = make_in_maps(inputs)
    res = run_bass_kernel_spmd(nc, in_maps, core_ids=list(range(8)))
    out = np.empty((8192, C), dtype=np.float32)
    for c in range(8):
        out[c * T:(c + 1) * T] = res.results[c]["outT"].T
    return out.reshape(2, 4096, C)


# version 15 (final)
# speedup vs baseline: 1.0040x; 1.0040x over previous
import contextlib
from collections import defaultdict

import numpy as np
import concourse.bass as bass
import concourse.mybir as mybir
from concourse.bass_utils import run_bass_kernel_spmd

F32 = mybir.dt.float32
BF16 = mybir.dt.bfloat16
AF = mybir.ActivationFunctionType
ALU = mybir.AluOpType
AX = mybir.AxisListType

T = 1024
C = 2048
CC = 16
H = 16
TC = 8
NWCOL = 9
NWKQ = 4

SEMNAMES = ["ld", "sx", "st", "te", "tkv", "tg", "tq", "tu", "ty", "tf",
            "ve", "se", "cc"]


class Ctx:
    def __init__(self, nc, sems, cur):
        self.nc = nc
        self.sems = sems
        self.cur = cur
        self.cnt = defaultdict(int)

    def emit(self, e, fn, *a, sem=None, inc=1, **k):
        i = getattr(getattr(self.nc, e), fn)(*a, **k) if self.cur == e else None
        if sem is not None:
            self.cnt[sem] += inc
            if i is not None:
                i.then_inc(self.sems[sem], inc)
        return self.cnt[sem] if sem is not None else None

    def wait(self, e, sem, val):
        if val <= 0:
            return
        if self.cur == e:
            getattr(self.nc, e).wait_ge(self.sems[sem], val)

    def gdma(self, out, in_):
        return self.emit("gpsimd", "dma_start", sem="ld", inc=16, out=out, in_=in_)

    def gdummy(self):
        return self.emit("gpsimd", "dma_start", sem="ld", inc=16,
                         out=self.dumap, in_=self.dumsrc)

    def sdma(self, out, in_, sem):
        return self.emit("sync", "dma_start", sem=sem, inc=16, out=out, in_=in_)

    def mm(self, out, lhsT, rhs, start, stop, sem=None):
        return self.emit("tensor", "matmul", out, lhsT, rhs,
                         sem=sem, start=start, stop=stop)

    def vspace(self, n=1):
        for _ in range(n):
            self.emit("vector", "tensor_copy", self.spcv, self.spcsrc)

    def sspace(self, n=1):
        for _ in range(n):
            self.emit("scalar", "copy", self.spcs, self.spcsrc)

    def vmark(self, sem="ve", n=2):
        for _ in range(n - 1):
            self.emit("vector", "tensor_copy", self.spcv, self.spcsrc)
        return self.emit("vector", "tensor_copy", self.spcv, self.spcsrc,
                         sem=sem)

    def smark(self, sem="se", n=2):
        for _ in range(n - 1):
            self.emit("scalar", "copy", self.spcs, self.spcsrc)
        return self.emit("scalar", "copy", self.spcs, self.spcsrc, sem=sem)


def build_program(taps=()):
    nc = bass.Bass("TRN2", target_bir_lowering=False)

    dr = {}
    dr["xT"] = nc.dram_tensor("xT", [C, T], F32, kind="ExternalInput").ap()
    for n in ("wqT", "wkT", "wvT", "woT", "wupT", "wdT", "we"):
        dr[n] = nc.dram_tensor(n, [C, C], F32, kind="ExternalInput").ap()
    dr["wgT"] = nc.dram_tensor("wgT", [C, 8], F32, kind="ExternalInput").ap()
    dr["identf"] = nc.dram_tensor("identf", [128, 128], F32, kind="ExternalInput").ap()
    dr["bsel"] = nc.dram_tensor("bsel", [128, 2], F32, kind="ExternalInput").ap()
    dr["outT"] = nc.dram_tensor("outT", [C, T], F32, kind="ExternalOutput").ap()
    cc_in = nc.dram_tensor("cc_in", [128, 2 * 2064], F32).ap()
    cc_out = nc.dram_tensor("cc_out", [128, 2 * 2064], F32).ap()

    tshapes = {"xn": ([128, CC * T], BF16), "k": ([128, TC * C], BF16),
               "v": ([128, TC * C], BF16), "kvev": ([128, 2064], F32),
               "kvsel": ([128, 2064], F32), "q": ([128, CC * T], BF16),
               "h1": ([128, CC * T], BF16), "G": ([128, T], F32),
               "y": ([128, CC * T], BF16), "ltok": ([128, 64], F32),
               "gtok": ([128, 8], F32), "krep": ([128, H * 128], BF16),
               "ccin": ([128, 2 * 2064], F32), "ccout": ([128, 2 * 2064], F32),
               "stg": ([128, 2064], F32)}
    tapdr = {tp: nc.dram_tensor("dbg_" + tp, tshapes[tp][0], tshapes[tp][1],
                                kind="ExternalOutput").ap() for tp in taps}

    stack = contextlib.ExitStack()
    with stack:
        def sb(name, shape, dt, st=stack):
            return st.enter_context(nc.sbuf_tensor(name, shape, dt))

        regA0 = contextlib.ExitStack()
        xstg = sb("xstg", [128, 8 * T], F32, regA0)
        regA0.close()
        xn = sb("xn", [128, CC * T], BF16)
        kvbf = sb("kvbf", [128, H * 128], BF16)
        krep = sb("krep", [128, H * 128], BF16)
        kvsel = sb("kvsel", [128, 2064], F32)
        Gt = sb("Gt", [128, T], F32)
        wg = sb("wg", [128, CC * 8], BF16)
        onesb = sb("onesb", [128, 128], BF16)
        identf = sb("identf_sb", [128, 128], F32)
        onesf = sb("onesf", [1, 128], F32)
        bselT = sb("bselT", [128, 2], F32)
        gate = sb("gate", [128, 128], F32)
        fms = sb("fms", [128, 4 * 512], F32)
        grow = Gt[0:1, 0:T]
        epsb = sb("epsb", [128, 1], F32)
        spc = sb("spc", [128, 16], F32)
        dum = sb("dum", [128, 32], F32)
        psb = [stack.enter_context(nc.psum_tensor(f"ps{i}", [128, 512], F32))
               for i in range(8)]
        sems = {n: stack.enter_context(nc.semaphore(n)) for n in SEMNAMES}

        regB = contextlib.ExitStack()
        wkq = sb("wkq", [128, NWKQ * CC * 512], BF16, regB)
        kvev = sb("kvev", [128, 2064], F32, regB)
        stg = sb("stg", [128, 2064], F32, regB)
        regC1 = contextlib.ExitStack()
        xbf = sb("xbf", [128, CC * T], BF16, regC1)
        xsq = sb("xsq", [128, 2 * T], BF16, regC1)
        Rb = sb("Rb", [128, T], F32, regC1)
        MRb = sb("MRb", [128, T], F32, regC1)
        tmpr = sb("tmpr", [128, 2 * T], F32, regC1)
        regC1.close()
        regC2 = contextlib.ExitStack()
        ksb = sb("ksb", [128, TC * C], BF16, regC2)
        vsb = sb("vsb", [128, TC * 2064], BF16, regC2)
        regC2.close()
        regB.close()
        b_orig = NWKQ * CC * 512 * 2 + 2064 * 4 + 2064 * 4
        b_new = NWCOL * C * 2 + 2064 * 4 + CC * T * 2
        padb = b_orig - b_new
        assert padb >= 0 and padb % 32 == 0, padb
        wcol = sb("wcol", [128, NWCOL * C], BF16)
        scr9 = sb("scr9", [128, 2064], F32)
        ysb = sb("ysb", [128, CC * T], BF16)
        if padb:
            sb("padB", [128, padb // 4], F32)
        qsb = sb("qsb", [128, CC * T], BF16)
        h1sb = sb("h1sb", [128, CC * T], BF16)
        addr = lambda h: nc.lookup_mloc(h).addr
        assert addr(xstg) == addr(xn), (addr(xstg), addr(xn))
        assert addr(wcol) == addr(wkq), (addr(wcol), addr(wkq))
        assert addr(qsb) == addr(ksb), (addr(qsb), addr(ksb))
        assert addr(h1sb) == addr(vsb), (addr(h1sb), addr(vsb))

        for n in SEMNAMES:
            nc.sync.sem_clear(sems[n])

        def emit(ctx: Ctx):
            ctx.emit("vector", "memset", onesb[:], 1.0)
            ctx.emit("vector", "memset", epsb[:], 1e-6)
            consts_ve = ctx.emit("vector", "memset", onesf[:], 1.0, sem="ve")
            ctx.sdma(identf[:], dr["identf"], "sx")
            ctx.sdma(bselT[:], dr["bsel"], "sx")
            ctx.sdma(identf[:], dr["identf"], "sx")
            bsel_sx = ctx.sdma(bselT[:], dr["bsel"], "sx")

            def wg_load():
                return ctx.gdma(wg[:].rearrange("p (cc e) -> p cc e", cc=CC),
                                dr["wgT"].rearrange("(cc p) e -> p cc e", p=128))

            wg_load()
            order = list(range(7, CC)) + list(range(7))
            gp_chunks = [cc for i, cc in enumerate(order) if i % 2 == 0]
            sy_chunks = [cc for i, cc in enumerate(order) if i % 2 == 1]
            sslot = {cc: j for j, cc in enumerate(sy_chunks)}
            NDUP = 3
            for _ in range(2):
                ctx.gdummy()
                ctx.sdma(dum[:], dr["identf"][:, 0:32], "sx")
            for cc in gp_chunks[:NDUP]:
                ctx.gdma(xbf[:, cc * T:(cc + 1) * T],
                         dr["xT"][cc * 128:(cc + 1) * 128, :])
            for cc in sy_chunks[:NDUP]:
                ctx.sdma(xstg[:, sslot[cc] * T:(sslot[cc] + 1) * T],
                         dr["xT"][cc * 128:(cc + 1) * 128, :], "sx")
            xv = {}
            for cc in gp_chunks:
                xv[cc] = ctx.gdma(xbf[:, cc * T:(cc + 1) * T],
                                  dr["xT"][cc * 128:(cc + 1) * 128, :])
            for cc in sy_chunks:
                xv[cc] = ctx.sdma(xstg[:, sslot[cc] * T:(sslot[cc] + 1) * T],
                                  dr["xT"][cc * 128:(cc + 1) * 128, :], "sx")
            wg_ld = wg_load()
            sx_tail = ctx.sdma(dum[:], dr["identf"][:, 0:32], "sx")
            xwait = {}
            for i, cc in enumerate(gp_chunks):
                nxt = xv[gp_chunks[i + 1]] if i + 1 < len(gp_chunks) else wg_ld
                xwait[cc] = ("ld", nxt)
            for i, cc in enumerate(sy_chunks):
                nxt = xv[sy_chunks[i + 1]] if i + 1 < len(sy_chunks) else sx_tail
                xwait[cc] = ("sx", nxt)

            sq_se = {}
            for i, cc in enumerate(order):
                ctx.wait("scalar", *xwait[cc])
                if i >= 2:
                    ctx.wait("scalar", "te", i - 1)
                if cc in sslot:
                    j = sslot[cc]
                    ctx.emit("scalar", "copy", xbf[:, cc * T:(cc + 1) * T],
                             xstg[:, j * T:(j + 1) * T])
                    ctx.emit("scalar", "activation",
                             xsq[:, (i % 2) * T:(i % 2 + 1) * T],
                             xstg[:, j * T:(j + 1) * T], AF.Square)
                else:
                    ctx.emit("scalar", "activation",
                             xsq[:, (i % 2) * T:(i % 2 + 1) * T],
                             xbf[:, cc * T:(cc + 1) * T], AF.Square)
                sq_se[i] = ctx.smark(n=4)

            ctx.wait("tensor", "ve", consts_ve)
            for i, cc in enumerate(order):
                ctx.wait("tensor", "se", sq_se[i])
                ctx.wait("tensor", *xwait[cc])
                for tt in range(2):
                    ctx.mm(psb[tt][0:1, :], onesb[:, 0:1],
                           xbf[:, cc * T + tt * 512: cc * T + (tt + 1) * 512],
                           start=(i == 0), stop=(i == CC - 1))
                    ctx.mm(psb[2 + tt][0:1, :], onesb[:, 0:1],
                           xsq[:, (i % 2) * T + tt * 512:(i % 2) * T + (tt + 1) * 512],
                           start=(i == 0), stop=(i == CC - 1),
                           sem="te" if tt == 1 else None)
            stats_te = ctx.cnt["te"]
            assert stats_te == CC

            A = fms[0:1, 0:T]
            B = fms[0:1, T:2 * T]
            TMP = MRb[0:1, 0:T]
            ctx.wait("vector", "te", stats_te)
            for tt in range(2):
                ctx.emit("vector", "tensor_scalar_mul",
                         A[0:1, tt * 512:(tt + 1) * 512], psb[tt][0:1, :], 1.0 / C)
            for tt in range(2):
                ctx.emit("vector", "tensor_scalar_mul",
                         B[0:1, tt * 512:(tt + 1) * 512],
                         psb[2 + tt][0:1, :], 1.0 / C)
            ctx.emit("vector", "tensor_mul", TMP, A, A)
            ctx.vspace(2)
            ctx.emit("vector", "tensor_sub", B, B, TMP)
            ctx.vspace(2)
            ctx.emit("vector", "tensor_scalar_add", B, B, 1e-5)
            var_ve = ctx.vmark(n=4)
            ctx.wait("scalar", "ve", var_ve)
            ctx.emit("scalar", "activation", B, B, AF.Sqrt)
            sd_se = ctx.smark(n=4)
            ctx.wait("vector", "se", sd_se)
            ctx.emit("vector", "reciprocal", B, B)
            ctx.vspace(2)
            ctx.emit("vector", "tensor_mul", TMP, A, B)
            rows_ve = ctx.vmark(n=4)

            ctx.wait("tensor", "ve", rows_ve)
            for tt in range(2):
                ctx.mm(psb[4 + tt][:, :], onesf[:],
                       B[0:1, tt * 512:(tt + 1) * 512], True, True)
                ctx.mm(psb[6 + tt][:, :], onesf[:],
                       TMP[0:1, tt * 512:(tt + 1) * 512], True, True,
                       sem="te" if tt == 1 else None)
            bc_te = ctx.cnt["te"]
            assert bc_te == CC + 1
            ctx.wait("scalar", "te", bc_te)
            for tt in range(2):
                ctx.emit("scalar", "copy", Rb[:, tt * 512:(tt + 1) * 512],
                         psb[4 + tt][:, :])
                ctx.emit("scalar", "copy", MRb[:, tt * 512:(tt + 1) * 512],
                         psb[6 + tt][:, :])
            rbmrb_se = ctx.smark(n=4)

            ctx.wait("vector", "se", rbmrb_se)
            xn_tc = {}
            for tc in range(TC):
                rbs = Rb[:, tc * 128:(tc + 1) * 128]
                mrs = MRb[:, tc * 128:(tc + 1) * 128]
                for cp in range(CC // 2):
                    c0, c1 = 2 * cp, 2 * cp + 1
                    o0 = c0 * T + tc * 128
                    o1 = c1 * T + tc * 128
                    ctx.emit("vector", "tensor_mul", tmpr[:, 0:128],
                             xbf[:, o0:o0 + 128], rbs)
                    ctx.emit("vector", "tensor_mul", tmpr[:, 128:256],
                             xbf[:, o1:o1 + 128], rbs)
                    ctx.emit("vector", "tensor_sub", xn[:, o0:o0 + 128],
                             tmpr[:, 0:128], mrs)
                    ctx.emit("vector", "tensor_sub", xn[:, o1:o1 + 128],
                             tmpr[:, 128:256], mrs)
                xn_tc[tc] = ctx.vmark()
            xn_ve = xn_tc[TC - 1]
            for tc in range(TC):
                ctx.emit("vector", "memset",
                         vsb[:, tc * 2064:(tc + 1) * 2064].rearrange(
                             "p (h e) -> p h e", h=H)[:, :, 128:129], 1.0)
            vones_ve = ctx.vmark()
            if "xn" in tapdr:
                ctx.wait("sync", "ve", xn_ve)
                ctx.sdma(tapdr["xn"], xn[:], "st")

            P2TE0 = CC + 1
            shld = {}

            def p2_sub_done(w, sh):
                return P2TE0 + (w * 4 + sh + 1) * TC

            shcush = {}
            for w, wname in enumerate(("wkT", "wvT")):
                for sh in range(4):
                    bi = w * 4 + sh
                    if bi >= NWKQ:
                        pw, psh = divmod(bi - NWKQ, 4)
                        ctx.wait("gpsimd", "te", p2_sub_done(pw, psh))
                        ctx.gdummy()
                    slot = bi % NWKQ
                    src = dr[wname][:, sh * 512:(sh + 1) * 512].rearrange(
                        "(cc p) o -> p cc o", p=128)
                    dst = wkq[:, slot * CC * 512:(slot + 1) * CC * 512].rearrange(
                        "p (cc o) -> p cc o", cc=CC)
                    shld[(w, sh)] = ctx.gdma(dst, src)
                    shcush[(w, sh)] = ctx.gdma(dum[:], dr["identf"][:, 0:32])
            tail_kv_ld = ctx.cnt["ld"]

            bankfree = {}
            kadd_ve = {}
            NROT = 6
            for G in range(64):
                w, sh, tc = G // 32, (G // 8) % 4, G % 8
                ctx.wait("tensor", "ld", shcush[(w, sh)])
                if G == 0:
                    ctx.wait("tensor", "ld", wg_ld)
                if w == 0 and sh < 2:
                    ctx.wait("tensor", "ve", xn_tc[tc])
                if (w, sh, tc) == (0, 2, 0):
                    ctx.wait("tensor", "ve", xn_ve)
                if G >= NROT:
                    sname, v = bankfree[G - NROT]
                    ctx.wait("tensor", sname, v)
                bank = G % NROT
                slot = (w * 4 + sh) % NWKQ
                for cc in range(CC):
                    lhsT = xn[:, cc * T + tc * 128: cc * T + tc * 128 + 128]
                    s0 = (slot * CC + cc) * 512
                    last = (cc == CC - 1)
                    if w == 0 and sh == 0:
                        ctx.mm(psb[7][:, tc * 8:(tc + 1) * 8], lhsT,
                               wg[:, cc * 8:(cc + 1) * 8],
                               start=(cc == 0), stop=last)
                    ctx.mm(psb[bank][:, :], lhsT, wkq[:, s0:s0 + 512],
                           start=(cc == 0), stop=last,
                           sem="te" if last else None)
                gte = ctx.cnt["te"]
                assert gte == P2TE0 + G + 1

                dst_off = tc * C + sh * 512
                if w == 0:
                    ctx.wait("scalar", "te", gte)
                    ctx.wait("scalar", "ve", xn_ve)
                    if G >= 2 and (G - 2) in kadd_ve:
                        ctx.wait("scalar", "ve", kadd_ve[G - 2])
                    base = (G % 2) * 2 * 512
                    rS = fms[:, base:base + 512]
                    pS = fms[:, base + 512:base + 1024]
                    ctx.emit("scalar", "activation", rS, psb[bank][:, :],
                             AF.Relu, scale=-1.0)
                    ctx.emit("scalar", "activation", pS, psb[bank][:, :],
                             AF.Relu)
                    ctx.emit("scalar", "activation", rS, rS, AF.Exp,
                             scale=-1.0)
                    kse = ctx.smark()
                    ctx.wait("vector", "se", kse)
                    ctx.emit("vector", "tensor_add",
                             ksb[:, dst_off:dst_off + 512], rS, pS)
                    kadd_ve[G] = ctx.vmark()
                    bankfree[G] = ("se", kse)
                else:
                    ctx.wait("vector", "te", gte)
                    vdst = vsb[:, tc * 2064 + sh * 4 * 129:
                               tc * 2064 + (sh * 4 + 4) * 129].rearrange(
                        "p (h e) -> p h e", h=4)[:, :, 0:128]
                    vsrc = psb[bank][:, :].rearrange("p (h e) -> p h e", h=4)
                    ctx.emit("vector", "tensor_copy", vdst, vsrc, sem="ve")
                    bankfree[G] = ("ve", ctx.cnt["ve"])
            assert ctx.cnt["te"] == P2TE0 + 64
            kv_inputs_ve = ctx.vmark()

            if "k" in tapdr:
                ctx.wait("sync", "ve", kv_inputs_ve)
                ctx.sdma(tapdr["k"], ksb[:], "st")
            if "v" in tapdr:
                ctx.wait("sync", "ve", kv_inputs_ve)
                ctx.sdma(tapdr["v"], vsb[:], "st")

            LT = gate[:, 0:64]
            M1, M2 = gate[:, 64:72], gate[:, 72:80]
            DF, E2 = gate[:, 80:88], gate[:, 88:96]
            U, GK, MS = gate[:, 96:104], gate[:, 104:112], gate[:, 112:120]
            gate_te = P2TE0 + 8
            ctx.wait("vector", "te", gate_te)
            ltok_ve = ctx.emit("vector", "tensor_copy", LT, psb[7][:, 0:64],
                               sem="ve")
            if "ltok" in tapdr:
                ctx.wait("sync", "ve", ltok_ve)
                ctx.sdma(tapdr["ltok"], gate[:, 0:64], "st")
            MSt = [fms[:, 8 * tc: 8 * (tc + 1)] for tc in range(TC)]
            for tc in range(TC):
                ctx.emit("vector", "reduce_max", M1[:, tc:tc + 1],
                         gate[:, tc * 8:(tc + 1) * 8], AX.X)
            for tc in range(TC):
                ctx.emit("vector", "tensor_scalar", MSt[tc],
                         gate[:, tc * 8:(tc + 1) * 8], M1[:, tc:tc + 1],
                         None, ALU.is_ge)
            for tc in range(TC):
                ctx.emit("vector", "scalar_tensor_tensor", MSt[tc], MSt[tc],
                         -1e30, gate[:, tc * 8:(tc + 1) * 8],
                         ALU.mult, ALU.add)
            for tc in range(TC):
                ctx.emit("vector", "reduce_max", M2[:, tc:tc + 1], MSt[tc], AX.X)
            ctx.vspace(2)
            ctx.emit("vector", "tensor_sub", DF, M2, M1)
            df_ve = ctx.vmark(n=4)
            ctx.wait("scalar", "ve", df_ve)
            ctx.emit("scalar", "activation", E2, DF, AF.Exp)
            e2_se = ctx.smark(n=4)
            ctx.wait("vector", "se", e2_se)
            ctx.emit("vector", "tensor_scalar_add", U, E2, 1.0)
            ctx.vspace(2)
            ctx.emit("vector", "reciprocal", U, U)
            ctx.vspace(2)
            ctx.emit("vector", "scalar_tensor_tensor", GK, E2, 1.0, U,
                     ALU.add, ALU.mult)
            gtok_ve = ctx.vmark(n=4)
            if "gtok" in tapdr:
                ctx.wait("sync", "ve", gtok_ve)
                ctx.sdma(tapdr["gtok"], gate[:, 104:112], "st")
            ctx.wait("tensor", "ve", kv_inputs_ve)
            ctx.wait("tensor", "ve", vones_ve)
            kvev_ve = {}
            for h in range(H):
                bank = h % 6
                if h >= 6:
                    ctx.wait("tensor", "ve", kvev_ve[h - 6])
                for tc in range(TC):
                    lhsT = ksb[:, tc * C + h * 128: tc * C + (h + 1) * 128]
                    ctx.mm(psb[bank][:, 0:129], lhsT,
                           vsb[:, tc * 2064 + h * 129: tc * 2064 + (h + 1) * 129],
                           start=(tc == 0), stop=(tc == TC - 1),
                           sem="tkv" if tc == TC - 1 else None)
                gkv = ctx.cnt["tkv"]
                ctx.wait("vector", "tkv", gkv)
                ctx.emit("vector", "tensor_copy",
                         kvev[:, h * 129:(h + 1) * 129], psb[bank][:, 0:129],
                         sem="ve")
                kvev_ve[h] = ctx.cnt["ve"]
                if h in (5, 11):
                    hf = 0 if h == 5 else 1
                    if hf == 0:
                        ctx.wait("tensor", "ve", gtok_ve)
                        ctx.wait("tensor", "sx", bsel_sx)
                        ctx.wait("tensor", "ve", ltok_ve)
                    else:
                        ctx.wait("tensor", "se", grow_se)
                    for tci in range(4):
                        tc = hf * 4 + tci
                        ctx.mm(psb[7][0:1, tci * 128:(tci + 1) * 128],
                               gate[:, 104 + tc:105 + tc], identf[:], True, True,
                               sem="tg" if tci == 3 else None)
                    gid = ctx.cnt["tg"]
                    ctx.wait("scalar", "tg", gid)
                    ctx.emit("scalar", "copy",
                             grow[:, hf * 512:(hf + 1) * 512], psb[7][0:1, :])
                    grow_se = ctx.smark(n=4)
            assert ctx.cnt["tkv"] == H
            kvev_all_ve = ctx.vmark()
            if "kvev" in tapdr:
                ctx.wait("sync", "ve", kvev_all_ve)
                ctx.sdma(tapdr["kvev"], kvev[:], "st")


            ctx.wait("tensor", "se", grow_se)
            ctx.wait("tensor", "ve", kvev_all_ve)
            ctx.mm(psb[4][:, :], onesf[:], grow[:, 0:512], True, True)
            ctx.mm(psb[5][:, :], onesf[:], grow[:, 512:1024], True, True,
                   sem="tg")
            gb_te = ctx.cnt["tg"]
            assert gb_te == 3
            ctx.wait("scalar", "tg", gb_te)
            ctx.emit("scalar", "copy", Gt[:, 0:512], psb[4][:, :])
            ctx.emit("scalar", "copy", Gt[:, 512:1024], psb[5][:, :])
            Gt_se = ctx.smark(n=4)
            if "G" in tapdr:
                ctx.wait("sync", "se", Gt_se)
                ctx.sdma(tapdr["G"], Gt[:], "st")

            blocks = ([("wqT", oc) for oc in range(CC)]
                      + [("wupT", oc) for oc in range(CC)]
                      + [(nm, oc) for oc in range(CC)
                         for nm in ("woT", "wdT", "we")])
            nq = nu = CC

            def block_consumed(bi):
                if bi < nq:
                    return ("tq", 2 * (bi + 1))
                if bi < nq + nu:
                    return ("tu", 2 * (bi - nq + 1))
                return ("tf", 2 * ((bi - nq - nu) // 3 + 1))

            blk_ld = {}
            cc_done = kvsel_ld = h0tmp_ld = None
            stg0_ve = stg1_ve = None
            vdone_te = p2_sub_done(1, 3)
            def wcol_load(bi):
                nm, oc = blocks[bi]
                src = dr[nm][:, oc * 128:(oc + 1) * 128].rearrange(
                    "(cc p) o -> p cc o", p=128)
                dst = wcol[:, (bi % NWCOL) * C:(bi % NWCOL + 1) * C].rearrange(
                    "p (cc o) -> p cc o", cc=CC)
                return ctx.gdma(dst, src)

            for bi, (nm, oc) in enumerate(blocks):
                if bi < NWCOL:
                    sh_owner = [0, 0, 0, 0, 1, 1, 1, 1, 2][bi]
                    ctx.wait("gpsimd", "te", p2_sub_done(1, sh_owner))
                    if bi == 0:
                        ctx.gdummy()
                if bi >= NWCOL:
                    sname, v = block_consumed(bi - NWCOL)
                    ctx.wait("gpsimd", sname, v)
                    ctx.gdummy()
                blk_ld[bi] = wcol_load(bi)
                if bi == 6:
                    ctx.wait("vector", "sx", bsel_sx)
                    ctx.emit("vector", "tensor_scalar_mul", stg[:],
                             kvev[:], bselT[:, 0:1], sem=None)
                    stg0_ve = ctx.vmark()
                    ctx.wait("gpsimd", "ve", stg0_ve)
                    ctx.gdummy()
                    ctx.gdma(cc_in[:, 0:2064], stg[:])
                    cid0 = ctx.gdma(cc_in[:, 0:2064], stg[:])
                    ctx.wait("vector", "ld", cid0)
                    ctx.emit("vector", "tensor_scalar_mul", stg[:],
                             kvev[:], bselT[:, 1:2], sem=None)
                    stg1_ve = ctx.vmark()
                    if "stg" in tapdr:
                        ctx.wait("sync", "ve", stg1_ve)
                        ctx.sdma(tapdr["stg"], stg[:], "st")
                    ctx.wait("gpsimd", "ve", stg1_ve)
                    ctx.gdma(cc_in[:, 2064:4128], stg[:])
                    cid1 = ctx.gdma(cc_in[:, 2064:4128], stg[:])
                    ctx.wait("gpsimd", "ld", cid1)
                    cc_done = ctx.emit("gpsimd", "collective_compute",
                                       "AllReduce", ALU.add,
                                       replica_groups=[list(range(8))],
                                       ins=[cc_in], outs=[cc_out],
                                       sem="cc", inc=1)
                if bi == nq + 1:
                    ctx.wait("gpsimd", "cc", cc_done)
                    ctx.gdummy()
                    ctx.gdma(kvsel[:], cc_out[:, 0:2064])
                    ctx.gdma(scr9[:], cc_out[:, 2064:4128])
                    kvsel_ld = ctx.gdma(kvsel[:], cc_out[:, 0:2064])
                    h0tmp_ld = ctx.gdma(scr9[:], cc_out[:, 2064:4128])
                    if "ccin" in tapdr:
                        ctx.gdma(tapdr["ccin"], cc_in)
                    if "ccout" in tapdr:
                        ctx.gdma(tapdr["ccout"], cc_out)
            tail_ld = wg_load()

            qadd_ve = {}
            qp3_se = {}
            for j in range(32):
                oc, tt = j // 2, j % 2
                bi = oc
                cush = blk_ld[bi + 1] if bi + 1 < len(blocks) else tail_ld
                ctx.wait("tensor", "ld", cush)
                if j >= 4:
                    ctx.wait("tensor", "se", qp3_se[j - 4])
                else:
                    sname, v = bankfree[60 + j]
                    ctx.wait("tensor", sname, v)
                bank = j % 4
                for cc in range(CC):
                    lhsT = wcol[:, (bi % NWCOL) * C + cc * 128:
                                (bi % NWCOL) * C + (cc + 1) * 128]
                    ctx.mm(psb[bank][:, :], lhsT,
                           xn[:, cc * T + tt * 512: cc * T + (tt + 1) * 512],
                           start=(cc == 0), stop=(cc == CC - 1),
                           sem="tq" if cc == CC - 1 else None)
                jte = ctx.cnt["tq"]
                ctx.wait("scalar", "tq", jte)
                if j >= 2:
                    ctx.wait("scalar", "ve", qadd_ve[j - 2])
                base = (j % 2) * 2 * 512
                rS = fms[:, base:base + 512]
                pS = fms[:, base + 512:base + 1024]
                ctx.emit("scalar", "activation", rS, psb[bank][:, :],
                         AF.Relu, scale=-1.0)
                ctx.emit("scalar", "activation", pS, psb[bank][:, :], AF.Relu)
                ctx.emit("scalar", "activation", rS, rS, AF.Exp, scale=-1.0)
                qp3_se[j] = ctx.smark()
                ctx.wait("vector", "se", qp3_se[j])
                qadd_ve[j] = ctx.emit(
                    "vector", "tensor_add",
                    qsb[:, oc * T + tt * 512: oc * T + (tt + 1) * 512],
                    rS, pS, sem="ve")
                if j == 27:
                    ctx.wait("vector", "ld", h0tmp_ld)
                    ctx.emit("vector", "tensor_scalar_mul", kvsel[:], kvsel[:],
                             bselT[:, 0:1])
                    ctx.vspace(2)
                    sel_ve = ctx.emit("vector", "scalar_tensor_tensor", kvsel[:],
                                      scr9[:], bselT[:, 1:2], kvsel[:],
                                      ALU.mult, ALU.add, sem="ve")
                    ctx.vspace(2)
                    ctx.emit("vector", "tensor_copy",
                             kvbf[:].rearrange("p (h e) -> p h e", h=H),
                             kvsel[:].rearrange("p (h e) -> p h e", h=H)[:, :, 0:128])
                    for h in range(H):
                        ctx.emit("vector", "tensor_scalar_mul",
                                 krep[:, h * 128:(h + 1) * 128], onesb[:],
                                 kvsel[:, h * 129 + 128: h * 129 + 129])
                    kv_ready_ve = ctx.vmark(n=4)
                    if "kvsel" in tapdr:
                        ctx.wait("sync", "ve", kv_ready_ve)
                        ctx.sdma(tapdr["kvsel"], kvsel[:], "st")
                    if "krep" in tapdr:
                        ctx.wait("sync", "ve", kv_ready_ve)
                        ctx.sdma(tapdr["krep"], krep[:], "st")


            assert ctx.cnt["tq"] == 32
            q_ve = ctx.cnt["ve"]
            if "q" in tapdr:
                ctx.wait("sync", "ve", q_ve)
                ctx.sdma(tapdr["q"], qsb[:], "st")

            h1_se = {}
            yev_ve = {}
            for j in range(32):
                oc, tt = j // 2, j % 2
                bi = nq + oc
                cush = blk_ld[bi + 1] if bi + 1 < len(blocks) else tail_ld
                ctx.wait("tensor", "ld", cush)
                if j >= 4:
                    ctx.wait("tensor", "se", h1_se[j - 4])
                else:
                    ctx.wait("tensor", "se", qp3_se[28 + j])
                bank = j % 4
                for cc in range(CC):
                    lhsT = wcol[:, (bi % NWCOL) * C + cc * 128:
                                (bi % NWCOL) * C + (cc + 1) * 128]
                    ctx.mm(psb[bank][:, :], lhsT,
                           xn[:, cc * T + tt * 512: cc * T + (tt + 1) * 512],
                           start=(cc == 0), stop=(cc == CC - 1),
                           sem="tu" if cc == CC - 1 else None)
                jte = ctx.cnt["tu"]
                ctx.wait("scalar", "tu", jte)
                h1_se[j] = ctx.emit(
                    "scalar", "activation",
                    h1sb[:, oc * T + tt * 512: oc * T + (tt + 1) * 512],
                    psb[bank][:, :], AF.Relu, sem="se")

                h, ptt = j // 2, j % 2
                p = j % 2
                if j == 0:
                    ctx.wait("tensor", "ve", kv_ready_ve)
                    ctx.wait("tensor", "se", Gt_se)
                if j >= 2:
                    ctx.wait("tensor", "ve", yev_ve[j - 2])
                qs = qsb[:, h * T + ptt * 512: h * T + (ptt + 1) * 512]
                ctx.mm(psb[4 + 2 * p][:, :], krep[:, h * 128:(h + 1) * 128],
                       qs, True, True)
                ite = ctx.mm(psb[5 + 2 * p][:, :],
                             kvbf[:, h * 128:(h + 1) * 128], qs, True, True,
                             sem="ty")
                ctx.wait("scalar", "ty", ite)
                zb = scr9[:, p * 512:(p + 1) * 512]
                ctx.emit("scalar", "add", zb, psb[4 + 2 * p][:, :], epsb[:])
                zse = ctx.smark()
                ctx.wait("vector", "se", zse)
                ctx.emit("vector", "reciprocal", zb, zb)
                ctx.vspace(1)
                yev_ve[j] = ctx.emit(
                    "vector", "tensor_mul",
                    ysb[:, h * T + ptt * 512: h * T + (ptt + 1) * 512],
                    psb[5 + 2 * p][:, :], zb, sem="ve")
            assert ctx.cnt["tu"] == 32
            assert ctx.cnt["ty"] == 32
            h1_all_se = ctx.smark()
            y_ve = ctx.vmark()
            if "h1" in tapdr:
                ctx.wait("sync", "se", h1_all_se)
                ctx.sdma(tapdr["h1"], h1sb[:], "st")
            if "y" in tapdr:
                ctx.wait("sync", "ve", y_ve)
                ctx.sdma(tapdr["y"], ysb[:], "st")

            ctx.wait("tensor", "ve", y_ve)
            ctx.wait("tensor", "se", h1_all_se)
            xres_sx = {}
            oev_ve = {}
            ost_st = {}
            for j in range(32):
                oc, tt = j // 2, j % 2
                if j == 0:
                    ctx.wait("sync", "ve", sel_ve)
                    ctx.sdma(dum[:], dr["identf"][:, 0:32], "sx")
                    for rep in range(2):
                        for j2 in range(2):
                            o2, t2 = j2 // 2, j2 % 2
                            xres_sx[j2] = ctx.sdma(
                                scr9[:, 1024 + j2 * 512:1024 + (j2 + 1) * 512],
                                dr["xT"][o2 * 128:(o2 + 1) * 128,
                                         t2 * 512:(t2 + 1) * 512], "sx")
                bq, bd = 2 * (j % 2), 2 * (j % 2) + 1
                if j >= 2:
                    ctx.wait("tensor", "ve", oev_ve[j - 2])
                bio = nq + nu + 3 * oc
                biu, bie = bio + 1, bio + 2
                cush = blk_ld[bie + 1] if bie + 1 < len(blocks) else tail_ld
                ctx.wait("tensor", "ld", cush)
                for cc in range(CC):
                    lhsT = wcol[:, (bio % NWCOL) * C + cc * 128:
                                (bio % NWCOL) * C + (cc + 1) * 128]
                    ctx.mm(psb[bq][:, :], lhsT,
                           ysb[:, cc * T + tt * 512: cc * T + (tt + 1) * 512],
                           start=(cc == 0), stop=False)
                for cc in range(CC):
                    lhsT = wcol[:, (biu % NWCOL) * C + cc * 128:
                                (biu % NWCOL) * C + (cc + 1) * 128]
                    ctx.mm(psb[bq][:, :], lhsT,
                           h1sb[:, cc * T + tt * 512: cc * T + (tt + 1) * 512],
                           start=False, stop=(cc == CC - 1))
                for cc in range(CC):
                    lhsT = wcol[:, (bie % NWCOL) * C + cc * 128:
                                (bie % NWCOL) * C + (cc + 1) * 128]
                    ctx.mm(psb[bd][:, :], lhsT,
                           xn[:, cc * T + tt * 512: cc * T + (tt + 1) * 512],
                           start=(cc == 0), stop=(cc == CC - 1),
                           sem="tf" if cc == CC - 1 else None)
                jte = ctx.cnt["tf"]

                ctx.wait("vector", "tf", jte)
                ctx.wait("vector", "sx", xres_sx[j])
                if j >= 2:
                    ctx.wait("vector", "st", ost_st[j - 2])
                if j == 0:
                    ctx.wait("vector", "se", Gt_se)
                ot = scr9[:, (j % 2) * 512:(j % 2 + 1) * 512]
                gslice = Gt[:, tt * 512:(tt + 1) * 512]
                ft = fms[:, ((j % 2) * 2) * 512:((j % 2) * 2) * 512 + 512]
                ctx.emit("vector", "tensor_add", ot, psb[bq][:, :],
                         scr9[:, 1024 + (j % 2) * 512:1024 + (j % 2 + 1) * 512])
                ctx.emit("vector", "tensor_mul", ft, psb[bd][:, :], gslice)
                ctx.vspace(1)
                ctx.emit("vector", "tensor_add", ot, ot, ft)
                oev_ve[j] = ctx.vmark()

                ctx.wait("sync", "ve", oev_ve[j])
                ctx.sdma(
                    dr["outT"][oc * 128:(oc + 1) * 128, tt * 512:(tt + 1) * 512],
                    ot, "st")
                ost_st[j] = ctx.sdma(
                    dr["outT"][oc * 128:(oc + 1) * 128, tt * 512:(tt + 1) * 512],
                    ot, "st")
                if j + 2 < 32:
                    o2, t2 = (j + 2) // 2, (j + 2) % 2
                    xres_sx[j + 2] = ctx.sdma(
                        scr9[:, 1024 + (j % 2) * 512:1024 + (j % 2 + 1) * 512],
                        dr["xT"][o2 * 128:(o2 + 1) * 128,
                                 t2 * 512:(t2 + 1) * 512], "sx")
            assert ctx.cnt["tf"] == 32
            ctx.wait("sync", "st", ctx.cnt["st"])

        def mkctx(name):
            c = Ctx(nc, sems, name)
            c.spcv = spc[:, 0:8]
            c.spcs = spc[:, 8:16]
            c.spcsrc = onesb[:, 0:8]
            c.dumap = dum[:]
            c.dumsrc = dr["identf"][:, 0:32]
            return c

        with nc.Block() as block:
            block.tensor(lambda _: emit(mkctx("tensor")))
            block.vector(lambda _: emit(mkctx("vector")))
            block.scalar(lambda _: emit(mkctx("scalar")))
            block.sync(lambda _: emit(mkctx("sync")))
            block.gpsimd(lambda _: emit(mkctx("gpsimd")))

    return nc


_NC_CACHE = {}


def _get_nc(taps=()):
    key = tuple(sorted(taps))
    if key not in _NC_CACHE:
        _NC_CACHE[key] = build_program(taps=key)
    return _NC_CACHE[key]


def make_in_maps(inputs):
    x = np.asarray(inputs["L0_attn_x"], dtype=np.float32)
    WqT = np.ascontiguousarray(np.asarray(inputs["L0_attn_Wq"], dtype=np.float32).T)
    WkT = np.ascontiguousarray(np.asarray(inputs["L0_attn_Wk"], dtype=np.float32).T)
    WvT = np.ascontiguousarray(np.asarray(inputs["L0_attn_Wv"], dtype=np.float32).T)
    WoT = np.ascontiguousarray(np.asarray(inputs["L0_attn_Wo"], dtype=np.float32).T)
    WupT = np.ascontiguousarray(np.asarray(inputs["L0_ffn_W_up"], dtype=np.float32).T)
    WdT = np.ascontiguousarray(np.asarray(inputs["L0_ffn_W_down"], dtype=np.float32).T)
    We = np.ascontiguousarray(np.asarray(inputs["L0_ffn_W_expert"], dtype=np.float32))
    WgT = np.ascontiguousarray(np.asarray(inputs["L0_ffn_W_gate"], dtype=np.float32).T)
    identf = np.eye(128, dtype=np.float32)
    xt = x.reshape(8192, C)
    in_maps = []
    for c in range(8):
        bsel = np.zeros((128, 2), dtype=np.float32)
        bsel[:, c // 4] = 1.0
        in_maps.append({
            "xT": np.ascontiguousarray(xt[c * T:(c + 1) * T].T),
            "wqT": WqT, "wkT": WkT, "wvT": WvT, "woT": WoT,
            "wupT": WupT, "wdT": WdT, "we": We, "wgT": WgT,
            "identf": identf, "bsel": bsel,
        })
    return in_maps


def kernel(**inputs):
    nc = _get_nc()
    in_maps = make_in_maps(inputs)
    res = run_bass_kernel_spmd(nc, in_maps, core_ids=list(range(8)))
    out = np.empty((8192, C), dtype=np.float32)
    for c in range(8):
        out[c * T:(c + 1) * T] = res.results[c]["outT"].T
    return out.reshape(2, 4096, C)
